# version 26
# baseline (speedup 1.0000x reference)
import numpy as np
import ml_dtypes
import concourse.bass as bass
import concourse.mybir as mybir
from concourse.bass_utils import run_bass_kernel_spmd

F32 = mybir.dt.float32
F32R = mybir.dt.float32r
BF16 = mybir.dt.bfloat16
AF = mybir.ActivationFunctionType
ALU = mybir.AluOpType
AXX = mybir.AxisListType.X

D = 2048
S = 2048
NT = 16
NH = 16
HD = 128
NCH = 16
EPS = 1e-5
NCORES = 8

BF = ml_dtypes.bfloat16


def _host_consts():
    j = np.arange(HD)
    ang = 2.0 * np.pi * (np.outer(j, j) % HD) / HD
    cmat = np.cos(ang).astype(np.float32).astype(BF)
    msin = (-np.sin(ang)).astype(np.float32).astype(BF)
    ident = np.eye(HD, dtype=BF)
    hblk = np.zeros((HD, NH * 16), dtype=BF)
    for h in range(NH):
        hblk[:, h * 16 + h] = 1.0
    sels = np.zeros((16, NH * HD), dtype=BF)
    for h in range(NH):
        sels[h, h * HD:(h + 1) * HD] = 1.0
    sel4 = np.zeros((4, 512), dtype=BF)
    for k in range(4):
        sel4[k, k * 128:(k + 1) * 128] = 1.0
    return dict(cmat=cmat, msin=msin, ident=ident, hblk=hblk, sels=sels,
                sel4=sel4)


def build_nc(debug=(), stop_after=None, cmode="full", dmode="full", dmode2="paux"):
    nc = bass.Bass("TRN2", target_bir_lowering=False,
                   dynamic_dma_scratch_size=16384)

    x_d = nc.dram_tensor("x", [S, D], F32, kind="ExternalInput")
    wqs_d = nc.dram_tensor("wqs", [NH, 128, D], F32, kind="ExternalInput")
    wkvs_d = nc.dram_tensor("wkvs", [NH, 128, D], F32, kind="ExternalInput")
    wot_d = nc.dram_tensor("wot", [D, D], F32, kind="ExternalInput")
    wt_d = nc.dram_tensor("wt", [D, D], F32, kind="ExternalInput")
    agw_d = nc.dram_tensor("agw", [128, NCH * 136], F32, kind="ExternalInput")
    bbt_d = nc.dram_tensor("bbt", [128, D], F32, kind="ExternalInput")
    brow4_d = nc.dram_tensor("brow4", [4, 512], F32, kind="ExternalInput")
    gb16_d = nc.dram_tensor("gb16", [16, 136], F32, kind="ExternalInput")
    cmat_d = nc.dram_tensor("cmat", [HD, HD], BF16, kind="ExternalInput")
    msin_d = nc.dram_tensor("msin", [HD, HD], BF16, kind="ExternalInput")
    ident_d = nc.dram_tensor("ident", [HD, HD], BF16, kind="ExternalInput")
    hblk_d = nc.dram_tensor("hblk", [HD, NH * 16], BF16, kind="ExternalInput")
    sels_d = nc.dram_tensor("sels", [16, NH * HD], BF16, kind="ExternalInput")
    sel4_d = nc.dram_tensor("sel4", [4, 512], BF16, kind="ExternalInput")

    x1s_d = nc.dram_tensor("x1s", [S, D], BF16, kind="Internal")
    wotb_d = nc.dram_tensor("wotb_s", [D, D], BF16, kind="Internal")
    wtb_d = nc.dram_tensor("wtb_s", [D, D], BF16, kind="Internal")
    out_d = nc.dram_tensor("out", [S, D], F32, kind="ExternalOutput")
    dbg_d = {}
    for name, shape, dt in [
        ("dbg_xct", [128, NCH * S], BF16), ("dbg_q", [128, NH * S], BF16),
        ("dbg_kv", [128, NH * S], BF16), ("dbg_invq", [16, S], BF16),
        ("dbg_invkv", [16, S], BF16), ("dbg_vh", [128, NH * S], BF16),
        ("dbg_x1", [S, D], BF16),
    ]:
        if name in debug:
            dbg_d[name] = nc.dram_tensor(name, shape, dt, kind="ExternalOutput")

    AR1 = nc.alloc_sbuf_tensor("AR1", [128, NH * S], BF16)
    AR2 = nc.alloc_sbuf_tensor("AR2", [128, NCH * S], BF16)
    AR3 = nc.alloc_sbuf_tensor("AR3", [128, NH * S], BF16)
    OV1 = nc.alloc_sbuf_tensor("OV1", [128, 6144], BF16)

    AR2_F = AR2.bitcast(F32)
    AR3_F = AR3.bitcast(F32)
    OV1_F = OV1.bitcast(F32)

    Q = AR1
    WTB = AR1
    XCT = AR2
    WOTB = AR2
    KV = AR3
    VH = AR3

    XRL = [AR3[:, 0:2048], AR3[:, 2048:4096]]
    AR1_F = AR1.bitcast(F32)
    XRL32 = AR1_F[:, 0:2048]
    XCC = [AR3[:, 4096:6144], AR3[:, 6144:8192]]

    STRIP = [OV1[:, 0:2048], OV1[:, 2048:4096]]
    SQ = [OV1[:, 4096:6144], OV1[:, 4096:6144]]
    QSTG = AR2_F[0:16, 2048:4096]
    KSTG = AR2_F[0:16, 0:2048]
    SELS = OV1[0:16, 0:2048]
    AGWB = OV1[:, 1024:3200]
    BBTB = OV1[:, 4096:6144]

    QN = AR2[:, 0:2048]
    KVN = AR2[:, 2048:4096]
    QF_RE = AR2[:, 4096:6144]
    QF_IM = AR2[:, 6144:8192]
    QV_RE = AR2[:, 8192:10240]
    QV_IM = AR2[:, 10240:12288]
    KVF_RE = AR2[:, 24576:26624]
    KVF_IM = AR2[:, 26624:28672]
    T1 = AR2[:, 20480:22528]
    T2 = AR2[:, 22528:24576]
    INVKV = AR2[0:16, 12288:14336]
    T1G = AR2[:, 28672:30720]
    T2G = AR2[:, 30720:32768]
    INVB = [OV1[:, 4096:5120], OV1[:, 5120:6144],
            AR2[:, 14336:15360], AR2[:, 15360:16384]]
    KVC_RE = AR2[:, 16384:18432]
    KVC_IM = AR2[:, 18432:20480]

    INVQ = OV1[0:16, 2048:4096]
    XRC = [AR1[:, 28672:30720], AR1[:, 30720:32768]]

    LN3D = [AR3[:, 0:2048], AR3[:, 26624:28672]]
    L3TD = [AR3[:, 2048:4096], AR3[:, 24576:26624]]
    XAS = AR3[:, 4096:4224]
    XAST = AR3[:, 4352:4480]
    OUTS_F = [AR3_F[:, 4096:6144], AR3_F[:, 6144:8192]]
    X1R = [AR3[:, 16384:18432], AR3[:, 18432:20480]]
    FFNT = [AR3[:, 20480:22528], AR3[:, 22528:24576]]

    CMAT = nc.alloc_sbuf_tensor("CMAT", [HD, HD], BF16)
    MSIN = nc.alloc_sbuf_tensor("MSIN", [HD, HD], BF16)
    IDENT = nc.alloc_sbuf_tensor("IDENT", [HD, HD], BF16)
    HBLK = nc.alloc_sbuf_tensor("HBLK", [HD, NH * 16], BF16)
    SEL4 = nc.alloc_sbuf_tensor("SEL4", [4, 512], BF16)
    BROW4 = nc.alloc_sbuf_tensor("BROW4", [4, 512], BF16)
    GB16 = nc.alloc_sbuf_tensor("GB16", [16, 136], BF16)
    STAT = nc.alloc_sbuf_tensor("STAT", [128, 24], F32)
    AGGR4 = nc.alloc_sbuf_tensor("AGGR4", [128, 4], F32)
    AGGR = AGGR4[:, 0:2]
    COLS = nc.alloc_sbuf_tensor("COLS", [128, 8], F32)
    LG = nc.alloc_sbuf_tensor("LG", [128, 8], F32)
    IS1 = nc.alloc_sbuf_tensor("IS1", [128, 8], F32)
    IS2 = nc.alloc_sbuf_tensor("IS2", [128, 8], F32)
    L2 = nc.alloc_sbuf_tensor("L2", [128, 8], F32)
    W8 = nc.alloc_sbuf_tensor("W8", [128, 8], F32)
    EPSC = AGGR4[:, 3:4]

    PMAIN = nc.alloc_psum_tensor("PMAIN", [128, 2048], F32)
    PAUX = nc.alloc_psum_tensor("PAUX", [128, 2048], F32)
    PMAIN_B = PMAIN.bitcast(BF16)
    PAUX_B = PAUX.bitcast(BF16)
    PTR = [PMAIN_B[:, 0:512], PMAIN_B[:, 1024:1536]]
    NORMS_Q = PAUX[0:16, :]
    NORMS_KV = PAUX[32:48, :]
    PXA = PMAIN[:, 1024:1152]
    PXAG = PMAIN[:, 1024:1160]
    PLG = PMAIN[:, 1152:1160]
    PL3T = [PMAIN_B[:, 0:512], PMAIN_B[:, 1024:1536]]
    PXAST = PMAIN_B[:, 2560:2688]

    sm = lambda n: nc.alloc_semaphore(n)
    s_const = sm("s_const"); s_const2 = sm("s_const2")
    s_xp = [sm("s_xp0"), sm("s_xp1")]
    s_m = sm("s_m"); s_ctr = sm("s_ctr")
    s_tr = sm("s_tr"); s_trev = sm("s_trev")
    s_sp = [sm("s_sp0"), sm("s_sp1")]
    s_qmm = sm("s_qmm"); s_qev = sm("s_qev"); s_qevl = sm("s_qevl")
    s_sq = sm("s_sq")
    s_normmm = sm("s_normmm")
    s_sels = sm("s_sels"); s_invrdy = sm("s_invrdy")
    s_fa = sm("s_fa"); s_fb = sm("s_fb"); s_scand = sm("s_scand")
    s_dft = sm("s_dft"); s_dftev = sm("s_dftev")
    s_cm = sm("s_cm"); s_ifft = sm("s_ifft"); s_vhev = sm("s_vhev")
    s_wot = sm("s_wot"); s_attn = sm("s_attn")
    s_wog = [sm("s_wog0"), sm("s_wog1"), sm("s_wog2"), sm("s_wog3")]
    s_stat = sm("s_stat"); s_sqrt = sm("s_sqrt")
    s_xrp = [sm("s_xrp0"), sm("s_xrp1")]
    s_res = sm("s_res")
    s_x1sp = [sm("s_x1sp0"), sm("s_x1sp1")]
    s_wt = sm("s_wt"); s_dsm = sm("s_dsm")
    s_x1rp = [sm("s_x1rp0"), sm("s_x1rp1")]
    s_l3 = sm("s_l3"); s_l3t = sm("s_l3t"); s_l3tev = sm("s_l3tev")
    s_gmm = sm("s_gmm"); s_xamm = sm("s_xamm"); s_gprod = sm("s_gprod")
    s_qkm = sm("s_qkm"); s_pcast = sm("s_pcast"); s_pcast2 = sm("s_pcast2")
    s_wogp = sm("s_wogp")
    s_gate1 = sm("s_gate1"); s_gate2 = sm("s_gate2")
    s_xas = sm("s_xas"); s_xat = sm("s_xat"); s_xatev = sm("s_xatev")
    s_ffnmm = sm("s_ffnmm")
    s_stat4 = sm("s_stat4"); s_sqrt4 = sm("s_sqrt4"); s_ffev = sm("s_ffev")
    s_st3 = sm("s_st3"); s_sq3 = sm("s_sq3")
    s_ln4ev = sm("s_ln4ev"); s_res4 = sm("s_res4")
    s_outp = [sm("s_outp0"), sm("s_outp1")]
    s_dbg = sm("s_dbg")

    with nc.Block("ph1") as blk:
        @blk.sync
        def _(sy):
            sy.dma_start(CMAT[:, :], cmat_d.ap()).then_inc(s_const, 16)
            sy.dma_start(MSIN[:, :], msin_d.ap()).then_inc(s_const, 16)
            sy.dma_start(IDENT[:, :], ident_d.ap()).then_inc(s_const, 16)
            sy.dma_start(HBLK[:, :], hblk_d.ap()).then_inc(s_const, 16)
            sy.dma_start(SEL4[:, :], sel4_d.ap()).then_inc(s_const, 16)
            for t in range(1, NT, 2):
                if t >= 2:
                    sy.wait_ge(s_m, t - 1)
                sy.dma_start(XRL32, x_d.ap()[t * 128:(t + 1) * 128, :]
                             ).then_inc(s_xp[1], 16)

        @blk.gpsimd
        def _(g):
            g.memset(COLS[:, 5:6], 0.0)
            g.memset(EPSC[:, :], EPS)
            g.dma_start(BROW4[:, :], brow4_d.ap()).then_inc(s_const2, 16)
            g.dma_start(GB16[:, :], gb16_d.ap()).then_inc(s_const2, 16)
            for t in range(0, NT, 2):
                if t >= 2:
                    g.wait_ge(s_m, t - 1)
                g.dma_start(XRL[0], x_d.ap()[t * 128:(t + 1) * 128, :]
                            ).then_inc(s_xp[0], 16)
            for h in (0, 1):
                g.dma_start(STRIP[h], wqs_d.ap()[h]).then_inc(s_sp[h], 16)

        @blk.scalar
        def _(a):
            for t in range(NT):
                a.wait_ge(s_xp[t % 2], 16 * (t // 2 + 1))
                if t >= 1:
                    a.wait_ge(s_ctr, t)
                if t >= 2:
                    a.wait_ge(s_tr, 4 * (t - 1))
                a.activation(XCC[t % 2], XRL[0] if t % 2 == 0 else XRL32,
                             AF.Identity,
                             accum_out=COLS[:, 3:4]).then_inc(s_m, 1)

        @blk.vector
        def _(v):
            for t in range(NT):
                v.wait_ge(s_m, t + 1)
                v.tensor_scalar_mul(COLS[:, 4:5], COLS[:, 3:4], 1.0 / D)
                v.drain()
                v.tensor_scalar(XCC[t % 2], XCC[t % 2], COLS[:, 4:5], None,
                                ALU.subtract).then_inc(s_ctr, 1)
                for grp in range(4):
                    gi = t * 4 + grp
                    v.wait_ge(s_tr, gi + 1)
                    src = PTR[gi % 2].rearrange("p (c n) -> p c n", c=4)
                    dst = XCT.rearrange("p (c n) -> p c n", c=NCH)[
                        :, 4 * grp:4 * grp + 4, t * 128:(t + 1) * 128]
                    v.tensor_copy(dst, src).then_inc(s_trev, 1)

        @blk.tensor
        def _(te):
            te.wait_ge(s_const, 16 * 5)
            for t in range(NT):
                te.wait_ge(s_ctr, t + 1)
                for grp in range(4):
                    gi = t * 4 + grp
                    if gi >= 2:
                        te.wait_ge(s_trev, gi - 1)
                    for cc in range(4):
                        c = grp * 4 + cc
                        ins = te.transpose(
                            PTR[gi % 2][:, cc * 128:(cc + 1) * 128],
                            XCC[t % 2][:, c * 128:(c + 1) * 128],
                            IDENT[:, :])
                        if cc == 3:
                            ins.then_inc(s_tr, 1)

    dbgn = [0]

    def _fin_dummy():
        with nc.Block("findummy") as blk:
            @blk.gpsimd
            def _(g):
                g.dma_start(out_d.ap()[0:128, 0:24], STAT[:, :]).then_inc(s_dbg, 16)
                g.wait_ge(s_dbg, dbgn[0] + 16)

    def dbg_dump(g, name, srcap):
        g.dma_start(dbg_d[name].ap(), srcap).then_inc(s_dbg, 16)
        dbgn[0] += 16

    if stop_after == "ph1":
        _fin_dummy()
        return nc

    if "dbg_xct" in dbg_d:
        with nc.Block("dbgxct") as blk:
            @blk.gpsimd
            def _(g):
                dbg_dump(g, "dbg_xct", XCT[:, :])
                g.wait_ge(s_dbg, dbgn[0])

    def b1_phase(blk, strip_dram, DEST, NORMS, base, prefetch_dram=None):

        @blk.gpsimd
        def _(g):
            for h in range(NH):
                if h < 2 and base["sp"] == 0:
                    continue
                if h == 0 and base["sp"] != 0:
                    continue
                if h >= 2:
                    g.wait_ge(s_qmm, base["qmm"] + h - 1)
                g.dma_start(STRIP[h % 2], strip_dram.ap()[h]
                            ).then_inc(s_sp[h % 2], 16)
            if prefetch_dram is not None:
                g.wait_ge(s_qmm, base["qmm"] + NH - 1)
                g.dma_start(STRIP[0], prefetch_dram.ap()[0]
                            ).then_inc(s_sp[0], 16)

        @blk.tensor
        def _(te):
            for h in range(NH):
                te.wait_ge(s_sp[h % 2], 16 * (base["sp"] + h // 2 + 1))
                if h >= 1:
                    te.wait_ge(s_qevl, base["qev"] + h)
                for c in range(NCH):
                    lhsT = STRIP[h % 2][:, c * 128:(c + 1) * 128]
                    for nn in range(4):
                        if h >= 1 and c == 0 and nn == 2:
                            te.wait_ge(s_qev, base["qev"] + h)
                        ins = te.matmul(PMAIN[:, nn * 512:(nn + 1) * 512], lhsT,
                                        XCT[:, c * 2048 + nn * 512:
                                            c * 2048 + (nn + 1) * 512],
                                        start=(c == 0), stop=(c == NCH - 1))
                        if c == NCH - 1 and nn == 3:
                            ins.then_inc(s_qmm, 1)
                if h >= 1:
                    te.wait_ge(s_sq, base["sq"] + h)
                    lhsT = HBLK[:, (h - 1) * 16:h * 16]
                    for nn in range(4):
                        ins = te.matmul(NORMS[:, nn * 512:(nn + 1) * 512], lhsT,
                                        SQ[(h - 1) % 2][:, nn * 512:(nn + 1) * 512],
                                        start=(h - 1 == 0), stop=False)
                    ins.then_inc(s_normmm, 1)
            te.wait_ge(s_sq, base["sq"] + NH)
            lhsT = HBLK[:, (NH - 1) * 16:NH * 16]
            for nn in range(4):
                ins = te.matmul(NORMS[:, nn * 512:(nn + 1) * 512], lhsT,
                                SQ[(NH - 1) % 2][:, nn * 512:(nn + 1) * 512],
                                start=False, stop=True)
            ins.then_inc(s_normmm, 1)

        @blk.scalar
        def _(a):
            for h in range(NH):
                a.wait_ge(s_qmm, base["qmm"] + h + 1)
                a.activation(DEST[:, h * 2048:h * 2048 + 1024],
                             PMAIN[:, 0:1024],
                             AF.Identity).then_inc(s_qevl, 1)
                a.activation(DEST[:, h * 2048 + 1024:(h + 1) * 2048],
                             PMAIN[:, 1024:2048],
                             AF.Identity).then_inc(s_qev, 1)

        @blk.vector
        def _(v):
            for h in range(NH):
                v.wait_ge(s_qev, base["qev"] + h + 1)
                if h >= 1:
                    v.wait_ge(s_normmm, base["norm"] + h)
                v.tensor_mul(SQ[h % 2], DEST[:, h * 2048:(h + 1) * 2048],
                             DEST[:, h * 2048:(h + 1) * 2048]).then_inc(s_sq, 1)

    with nc.Block("b1q") as blk:
        b1_phase(blk, wqs_d, Q, NORMS_Q,
                 dict(sp=0, qev=0, sq=0, qmm=0, norm=0), prefetch_dram=wkvs_d)

    if stop_after == "b1q":
        _fin_dummy()
        return nc

    with nc.Block("b1kv") as blk:
        b1_phase(blk, wkvs_d, KV, NORMS_KV,
                 dict(sp=NH // 2, qev=NH, sq=NH, qmm=NH, norm=NH))

    if stop_after == "b1kv":
        _fin_dummy()
        return nc

    with nc.Block("b1kv2") as blk:
        @blk.scalar
        def _(a):
            a.activation(QSTG, NORMS_Q[:, :], AF.Ln, scale=float(128 ** 3))
            a.activation(KSTG, NORMS_KV[:, :], AF.Ln, scale=128.0)
            a.drain()
            a.activation(INVQ, QSTG, AF.Exp, scale=-0.5).then_inc(s_invrdy, 1)
            a.activation(INVKV, KSTG, AF.Exp, scale=-0.5).then_inc(s_invrdy, 1)

    if "dbg_q" in dbg_d:
        with nc.Block("dbgq") as blk:
            @blk.gpsimd
            def _(g):
                dbg_dump(g, "dbg_q", Q[:, :])
                dbg_dump(g, "dbg_kv", KV[:, :])
                g.wait_ge(s_dbg, dbgn[0])

    with nc.Block("b2") as blk:
        @blk.sync
        def _(sy):
            sy.dma_start(SELS, sels_d.ap()).then_inc(s_sels, 16)

        @blk.gpsimd
        def _(g):
            for c in range(NCH):
                g.dma_start(wotb_d.ap()[c * 128:(c + 1) * 128, :],
                            wot_d.ap()[c * 128:(c + 1) * 128, :]
                            ).then_inc(s_pcast, 16)
            for c in range(NCH):
                g.dma_start(wtb_d.ap()[c * 128:(c + 1) * 128, :],
                            wt_d.ap()[c * 128:(c + 1) * 128, :]
                            ).then_inc(s_pcast2, 16)
            g.wait_ge(s_pcast, 16 * NCH)
            for c in (NCH - 2, NCH - 1):
                g.dma_start(WOTB[:, c * 2048:(c + 1) * 2048],
                            wotb_d.ap()[c * 128:(c + 1) * 128, :]
                            ).then_inc(s_wogp, 16)

        @blk.scalar
        def _(a):
            def vh_evac(a, hh):
                a.wait_ge(s_ifft, hh + 1)
                a.activation(VH[:, hh * 2048:(hh + 1) * 2048], PMAIN[:, :],
                             AF.Identity).then_inc(s_vhev, 1)

            sf = [s_fa, s_fb]
            for h in range(NH):
                for j in range(12):
                    p = j % 2
                    a.wait_ge(s_dft, 12 * h + j + 1)
                    if j < 4:
                        if h >= 1:
                            a.wait_ge(s_qkm, 4 * (h - 1) + j + 1)
                        a.activation(INVB[j],
                                     PAUX[:, p * 1024:(p + 1) * 1024],
                                     AF.Identity).then_inc(sf[p], 1)
                        continue
                    if j == 4 and h >= 1:
                        a.wait_ge(s_scand, h)
                    if j == 8 and h >= 1:
                        a.wait_ge(s_cm, h)
                    dst_t = [KVF_RE, KVF_RE, KVF_IM, KVF_IM,
                             QF_RE, QF_RE, QF_IM, QF_IM][j - 4]
                    a.activation(dst_t[:, p * 1024:(p + 1) * 1024],
                                 PAUX[:, p * 1024:(p + 1) * 1024],
                                 AF.Identity).then_inc(sf[p], 1)
                if h >= 1:
                    vh_evac(a, h - 1)
            vh_evac(a, NH - 1)

        @blk.tensor
        def _(te):
            te.wait_ge(s_sels, 16)
            te.wait_ge(s_invrdy, 2)

            def ifft(te, hh):
                te.wait_ge(s_cm, hh + 1)
                te.wait_ge(s_vhev, hh)
                for nn in range(4):
                    te.matmul(PMAIN[:, nn * 512:(nn + 1) * 512], CMAT[:, :],
                              QV_RE[:, nn * 512:(nn + 1) * 512],
                              start=True, stop=False)
                for nn in range(4):
                    ins = te.matmul(PMAIN[:, nn * 512:(nn + 1) * 512],
                                    MSIN[:, :],
                                    QV_IM[:, nn * 512:(nn + 1) * 512],
                                    start=False, stop=True)
                ins.then_inc(s_ifft, 1)

            sf = [s_fa, s_fb]
            for h in range(NH):
                selh = SELS[:, h * 128:(h + 1) * 128]
                for j in range(12):
                    p = j % 2
                    if 12 * h + j >= 2:
                        te.wait_ge(sf[p], 6 * h + j // 2)
                    if j in (4, 5):
                        te.wait_ge(s_qkm, 4 * h + 3 + (j - 4))
                    if j in (8, 9):
                        te.wait_ge(s_qkm, 4 * h + 1 + (j - 8))
                    if j < 4:
                        lhsT = selh
                        rhs_t = INVQ if j < 2 else INVKV
                        rhs = rhs_t[:, p * 1024:(p + 1) * 1024]
                    else:
                        lhsT = CMAT[:, :] if (j // 2) % 2 == 0 else MSIN[:, :]
                        srcb = KVN if j < 8 else QN
                        rhs = srcb[:, p * 1024:(p + 1) * 1024]
                    for nn in range(2):
                        o = p * 1024 + nn * 512
                        ins = te.matmul(PAUX[:, o:o + 512], lhsT,
                                        rhs[:, nn * 512:(nn + 1) * 512],
                                        start=True, stop=True)
                    ins.then_inc(s_dft, 1)
                if h >= 1:
                    ifft(te, h - 1)
            ifft(te, NH - 1)

        @blk.vector
        def _(v):
            zb = COLS[:, 5:6].broadcast_to([128, 2048])
            sf = [s_fa, s_fb]

            def midsec_a(v, h):
                v.wait_ge(s_fa, 6 * h + 4)
                v.wait_ge(s_fb, 6 * h + 4)
                v.tensor_tensor_scan(KVC_RE, KVF_RE, zb, 0.0, ALU.add, ALU.add)
                v.tensor_tensor_scan(KVC_IM, KVF_IM, zb, 0.0, ALU.add, ALU.add
                                     ).then_inc(s_scand, 1)
                v.drain()

            def midsec_b(v, h):
                v.wait_ge(s_fa, 6 * h + 6)
                v.wait_ge(s_fb, 6 * h + 6)
                v.tensor_mul(T1, QF_RE, KVC_RE)
                v.tensor_mul(T2, QF_IM, KVC_IM)
                v.drain()
                v.wait_ge(s_ifft, h)
                v.tensor_sub(QV_RE, T1, T2)
                v.drain()
                v.tensor_mul(T1, QF_RE, KVC_IM)
                v.tensor_mul(T2, QF_IM, KVC_RE)
                v.drain()
                v.tensor_add(QV_IM, T1, T2).then_inc(s_cm, 1)
                v.drain()

            for h in range(NH):
                if h >= 1:
                    midsec_a(v, h - 1)
                for j in range(4):
                    p = j % 2
                    v.wait_ge(sf[p], 6 * h + j // 2 + 1)
                    dst_t = QN if j < 2 else KVN
                    src_t = Q if j < 2 else KV
                    v.tensor_mul(dst_t[:, p * 1024:(p + 1) * 1024],
                                 src_t[:, h * 2048 + p * 1024:
                                       h * 2048 + (p + 1) * 1024],
                                 INVB[j]).then_inc(s_qkm, 1)
                if h >= 1:
                    midsec_b(v, h - 1)
            midsec_a(v, NH - 1)
            midsec_b(v, NH - 1)



    if stop_after == "b2":
        _fin_dummy()
        return nc

    if "dbg_vh" in dbg_d or "dbg_invq" in dbg_d:
        with nc.Block("dbgvh") as blk:
            @blk.gpsimd
            def _(g):
                if "dbg_vh" in dbg_d:
                    dbg_dump(g, "dbg_vh", VH[:, :])
                if "dbg_invq" in dbg_d:
                    dbg_dump(g, "dbg_invq", INVQ)
                    dbg_dump(g, "dbg_invkv", INVKV)
                g.wait_ge(s_dbg, dbgn[0])

    with nc.Block("c") as blk:
        @blk.gpsimd
        def _(g):
            for c in range(NCH - 2):
                g.dma_start(WOTB[:, c * 2048:(c + 1) * 2048],
                            wotb_d.ap()[c * 128:(c + 1) * 128, :]
                            ).then_inc(s_wog[c // 4], 16)
            g.dma_start(XRC[0], x_d.ap()[0:128, :]).then_inc(s_xrp[0], 16)
            g.dma_start(XRC[1], x_d.ap()[128:256, :]).then_inc(s_xrp[1], 16)
            g.wait_ge(s_pcast2, 16 * NCH)
            for t in range(NT):
                g.wait_ge(s_res, t + 1)
                g.dma_start(x1s_d.ap()[t * 128:(t + 1) * 128, :], XRC[t % 2]
                            ).then_inc(s_x1sp[t % 2], 16)
                if t + 2 < NT:
                    g.wait_ge(s_x1sp[t % 2], 16 * (t // 2 + 1))
                    g.dma_start(XRC[t % 2],
                                x_d.ap()[(t + 2) * 128:(t + 3) * 128, :]
                                ).then_inc(s_xrp[t % 2], 16)
                if t < NCH - 2:
                    g.dma_start(WTB[:, t * 2048:(t + 1) * 2048],
                                wtb_d.ap()[t * 128:(t + 1) * 128, :]
                                ).then_inc(s_wt, 16)
                if t == NCH - 2:
                    g.dma_start(AGWB, agw_d.ap()).then_inc(s_dsm, 16)
            g.wait_ge(s_attn, NT)
            for t in (0, 1):
                g.wait_ge(s_x1sp[t % 2], 16)
                g.dma_start(X1R[t], x1s_d.ap()[t * 128:(t + 1) * 128, :]
                            ).then_inc(s_x1rp[t], 16)

        @blk.tensor
        def _(te):
            for t in range(NT):
                pm = PMAIN if t % 2 == 0 else PAUX
                if t >= 2:
                    te.wait_ge(s_res, t - 1)
                order = ([NCH - 2, NCH - 1] + list(range(NCH - 2))) if t == 0 \
                    else list(range(NCH))
                for ci, c in enumerate(order):
                    if t == 0:
                        if ci == 0:
                            te.wait_ge(s_wogp, 32)
                        elif c % 4 == 0 and c < NCH - 4:
                            te.wait_ge(s_wog[c // 4], 64)
                        elif c == NCH - 4:
                            te.wait_ge(s_wog[3], 32)
                    lhsT = VH[:, c * 2048 + t * 128: c * 2048 + (t + 1) * 128]
                    for nn in range(4):
                        ins = te.matmul(pm[:, nn * 512:(nn + 1) * 512], lhsT,
                                        WOTB[:, c * 2048 + nn * 512:
                                             c * 2048 + (nn + 1) * 512],
                                        start=(ci == 0), stop=(ci == NCH - 1))
                ins.then_inc(s_attn, 1)

        @blk.vector
        def _(v):
            for t in range(NT):
                pm = PMAIN if t % 2 == 0 else PAUX
                v.wait_ge(s_attn, t + 1)
                if cmode == "nostat":
                    v.wait_ge(s_xrp[t % 2], 16 * (t // 2 + 1))
                    v.tensor_scalar(XRC[t % 2], XRC[t % 2], COLS[:, 5:6], None,
                                    ALU.add)
                    v.drain()
                    v.scalar_tensor_tensor(XRC[t % 2], pm[:, :], COLS[:, 5:6],
                                           XRC[t % 2], ALU.mult, ALU.add
                                           ).then_inc(s_res, 1)
                    continue
                if cmode == "min":
                    v.wait_ge(s_xrp[t % 2], 16 * (t // 2 + 1))
                    v.tensor_scalar(XRC[t % 2], XRC[t % 2], COLS[:, 5:6], None,
                                    ALU.add).then_inc(s_res, 1)
                    continue
                if t >= 1:
                    v.wait_ge(s_sqrt, t)
                for ch in range(4):
                    v.bn_stats(STAT[:, 6 * ch:6 * ch + 6],
                               pm[:, 512 * ch:512 * (ch + 1)])
                v.drain()
                v.bn_aggr(AGGR, STAT[:, :]).then_inc(s_stat, 1)
                v.wait_ge(s_sqrt, t + 1)
                v.reciprocal(COLS[:, 1:2], COLS[:, 0:1])
                v.drain()
                v.scalar_tensor_tensor(COLS[:, 2:3], AGGR4[:, 0:1], -1.0,
                                       COLS[:, 1:2], ALU.mult, ALU.mult)
                v.drain()
                v.wait_ge(s_xrp[t % 2], 16 * (t // 2 + 1))
                v.tensor_scalar(XRC[t % 2], XRC[t % 2], COLS[:, 2:3], None,
                                ALU.add)
                v.drain()
                if cmode == "nopsum":
                    v.tensor_scalar(XRC[t % 2], XRC[t % 2], COLS[:, 1:2], None,
                                    ALU.mult).then_inc(s_res, 1)
                else:
                    v.scalar_tensor_tensor(XRC[t % 2], pm[:, :], COLS[:, 1:2],
                                           XRC[t % 2], ALU.mult, ALU.add
                                           ).then_inc(s_res, 1)

        if cmode not in ("nostat", "min"):
            @blk.scalar
            def _(a):
                for t in range(NT):
                    a.wait_ge(s_stat, t + 1)
                    if t >= 1:
                        a.wait_ge(s_res, t)
                    a.activation(COLS[:, 0:1], AGGR4[:, 1:2], AF.Sqrt,
                                 bias=EPSC[:, :]).then_inc(s_sqrt, 1)

    if stop_after == "c":
        _fin_dummy()
        return nc

    if "dbg_x1" in dbg_d:
        with nc.Block("dbgx1") as blk:
            @blk.gpsimd
            def _(g):
                g.wait_ge(s_x1sp[0], 16 * 8)
                g.wait_ge(s_x1sp[1], 16 * 8)
                dbg_dump(g, "dbg_x1", x1s_d.ap())
                g.wait_ge(s_dbg, dbgn[0])

    with nc.Block("d") as blk:
        IS2v = LG
        LN4C = IS2

        @blk.gpsimd
        def _(g):
            g.wait_ge(s_x1sp[0], 16 * 8)
            g.wait_ge(s_x1sp[1], 16 * 8)
            for c in (NCH - 2, NCH - 1):
                g.dma_start(WTB[:, c * 2048:(c + 1) * 2048],
                            wtb_d.ap()[c * 128:(c + 1) * 128, :]
                            ).then_inc(s_wt, 16)
            g.dma_start(BBTB, bbt_d.ap()).then_inc(s_dsm, 16)
            for t in range(NT):
                if t >= 2:
                    g.wait_ge(s_res4, t - 1)
                    g.dma_start(X1R[t % 2],
                                x1s_d.ap()[t * 128:(t + 1) * 128, :]
                                ).then_inc(s_x1rp[t % 2], 16)
                if t >= 1:
                    g.wait_ge(s_res4, t)
                    g.dma_start(out_d.ap()[(t - 1) * 128:t * 128, :],
                                OUTS_F[(t - 1) % 2]).then_inc(s_outp[(t - 1) % 2], 16)
            g.wait_ge(s_res4, NT)
            g.dma_start(out_d.ap()[(NT - 1) * 128:NT * 128, :],
                        OUTS_F[(NT - 1) % 2]).then_inc(s_outp[(NT - 1) % 2], 16)

        @blk.vector
        def _(v):
            def gates_head(v, tt):
                v.wait_ge(s_gmm, tt + 1)
                v.tensor_copy(LG[:, :], PLG)
                v.drain()
                v.reduce_max(COLS[:, 5:6], LG[:, :], axis=AXX)
                v.drain()
                v.tensor_scalar(IS1[:, :], LG[:, :], COLS[:, 5:6], None,
                                ALU.is_ge)
                v.drain()
                v.scalar_tensor_tensor(L2[:, :], IS1[:, :], -1e30, LG[:, :],
                                       ALU.mult, ALU.add)
                v.drain()
                v.reduce_max(COLS[:, 6:7], L2[:, :], axis=AXX)
                v.drain()
                v.tensor_sub(COLS[:, 7:8], COLS[:, 6:7], COLS[:, 5:6]
                             ).then_inc(s_gate1, 1)
                v.wait_ge(s_gate2, tt + 1)
                v.tensor_scalar(COLS[:, 4:5], COLS[:, 3:4], -1.0, -1.0,
                                ALU.mult, ALU.subtract)
                v.drain()
                v.tensor_scalar(W8[:, :], IS1[:, :], COLS[:, 4:5], None,
                                ALU.mult)
                v.drain()
                v.tensor_scalar(IS2v[:, :], L2[:, :], COLS[:, 6:7], None,
                                ALU.is_ge)
                v.drain()
                v.scalar_tensor_tensor(W8[:, :], IS2v[:, :], COLS[:, 3:4],
                                       W8[:, :], ALU.mult, ALU.add)
                v.drain()
                v.tensor_mul(XAS.rearrange("p (a b) -> p a b", a=8),
                             PXA.rearrange("p (a b) -> p a b", a=8),
                             W8[:, :, None].broadcast_to([128, 8, 16])
                             ).then_inc(s_xas, 1)

            def xast_copy(v, tt):
                v.wait_ge(s_xat, tt + 1)
                v.tensor_copy(XAST, PXAST).then_inc(s_xatev, 1)

            def ln4_res(v, tt):
                v.wait_ge(s_ffev, tt + 1)
                if tt >= 1:
                    v.wait_ge(s_sqrt4, tt)
                for ch in range(4):
                    v.bn_stats(STAT[:, 6 * ch:6 * ch + 6],
                               FFNT[tt % 2][:, 512 * ch:512 * (ch + 1)])
                v.drain()
                v.bn_aggr(LN4C[:, 0:2], STAT[:, :]).then_inc(s_stat4, 1)
                v.wait_ge(s_sqrt4, tt + 1)
                v.reciprocal(LN4C[:, 3:4], LN4C[:, 2:3])
                v.drain()
                v.scalar_tensor_tensor(LN4C[:, 4:5], LN4C[:, 0:1], -1.0,
                                       LN4C[:, 3:4], ALU.mult, ALU.mult)
                v.drain()
                v.tensor_scalar(X1R[tt % 2], X1R[tt % 2], LN4C[:, 4:5], None,
                                ALU.add)
                v.drain()
                if tt >= 2:
                    v.wait_ge(s_outp[tt % 2], 16 * (tt // 2))
                v.scalar_tensor_tensor(OUTS_F[tt % 2], FFNT[tt % 2],
                                       LN4C[:, 3:4], X1R[tt % 2],
                                       ALU.mult, ALU.add).then_inc(s_res4, 1)

            for t in range(NT):
                if t >= 1:
                    gates_head(v, t - 1)
                v.wait_ge(s_x1rp[t % 2], 16 * (t // 2 + 1))
                if t >= 1:
                    v.wait_ge(s_sq3, t)
                for ch in range(4):
                    v.bn_stats(STAT[:, 6 * ch:6 * ch + 6],
                               X1R[t % 2][:, 512 * ch:512 * (ch + 1)])
                v.drain()
                v.bn_aggr(AGGR, STAT[:, :]).then_inc(s_st3, 1)
                v.wait_ge(s_sq3, t + 1)
                if t >= 1:
                    v.wait_ge(s_l3, 2 * t)
                v.reciprocal(COLS[:, 1:2], COLS[:, 0:1])
                v.drain()
                v.scalar_tensor_tensor(COLS[:, 2:3], AGGR4[:, 0:1], -1.0,
                                       COLS[:, 1:2], ALU.mult, ALU.mult
                                       ).then_inc(s_l3, 1)
                if t >= 1:
                    xast_copy(v, t - 1)
                if t >= 2:
                    v.wait_ge(s_ffnmm, t - 1)
                for grp in range(4):
                    v.wait_ge(s_l3t, 4 * t + grp + 1)
                    v.tensor_copy(L3TD[t % 2][:, grp * 512:(grp + 1) * 512],
                                  PL3T[grp % 2]).then_inc(s_l3tev, 1)
                if t >= 1:
                    ln4_res(v, t - 1)
            gates_head(v, NT - 1)
            xast_copy(v, NT - 1)
            ln4_res(v, NT - 1)

        @blk.scalar
        def _(a):
            for t in range(NT):
                if t >= 1:
                    tt = t - 1
                    a.wait_ge(s_gate1, tt + 1)
                    a.wait_ge(s_xas, tt)
                    a.activation(COLS[:, 3:4], COLS[:, 7:8], AF.Sigmoid
                                 ).then_inc(s_gate2, 1)
                a.wait_ge(s_st3, t + 1)
                a.activation(COLS[:, 0:1], AGGR4[:, 1:2], AF.Sqrt,
                             bias=EPSC).then_inc(s_sq3, 1)
                a.wait_ge(s_l3, 2 * t + 1)
                if t >= 2:
                    a.wait_ge(s_l3t, 4 * t - 4)
                a.activation(LN3D[t % 2], X1R[t % 2], AF.Identity,
                             scale=COLS[:, 1:2],
                             bias=COLS[:, 2:3]).then_inc(s_l3, 1)
                if t >= 1:
                    tt = t - 1
                    a.wait_ge(s_ffnmm, tt + 1)
                    if tt >= 2:
                        a.wait_ge(s_res4, tt - 1)
                    a.activation(FFNT[tt % 2], PAUX[:, :], AF.Identity
                                 ).then_inc(s_ffev, 1)
                    a.wait_ge(s_stat4, tt + 1)
                    a.activation(LN4C[:, 2:3], LN4C[:, 1:2], AF.Sqrt,
                                 bias=EPSC).then_inc(s_sqrt4, 1)
            tt = NT - 1
            a.wait_ge(s_gate1, tt + 1)
            a.wait_ge(s_xas, tt)
            a.activation(COLS[:, 3:4], COLS[:, 7:8], AF.Sigmoid
                         ).then_inc(s_gate2, 1)
            a.wait_ge(s_ffnmm, tt + 1)
            a.wait_ge(s_res4, tt - 1)
            a.activation(FFNT[tt % 2], PAUX[:, :], AF.Identity
                         ).then_inc(s_ffev, 1)
            a.wait_ge(s_stat4, tt + 1)
            a.activation(LN4C[:, 2:3], LN4C[:, 1:2], AF.Sqrt,
                         bias=EPSC).then_inc(s_sqrt4, 1)

        @blk.tensor
        def _(te):
            te.wait_ge(s_dsm, 32)
            te.wait_ge(s_const2, 32)
            te.wait_ge(s_wt, 16 * NCH)
            te.wait_ge(s_sels, 16)
            def l3_transposes(te, tt):
                te.wait_ge(s_l3, 2 * tt + 2)
                for grp in range(4):
                    if 4 * tt + grp >= 2:
                        te.wait_ge(s_l3tev, 4 * tt + grp - 1)
                    for cc in range(4):
                        c = grp * 4 + cc
                        ins = te.transpose(
                            PL3T[grp % 2][:, cc * 128:(cc + 1) * 128],
                            LN3D[tt % 2][:, c * 128:(c + 1) * 128],
                            IDENT[:, :])
                    ins.then_inc(s_l3t, 1)

            l3_transposes(te, 0)
            for t in range(NT):
                if t >= 1:
                    te.wait_ge(s_ffev, t)
                    te.wait_ge(s_xas, t)
                te.wait_ge(s_l3tev, 4 * t + 4)
                for c in range(NCH):
                    lhsT = L3TD[t % 2][:, c * 128:(c + 1) * 128]
                    te.matmul(PXAG, lhsT, AGWB[:, c * 136:(c + 1) * 136],
                              start=(c == 0), stop=False)
                te.matmul(PXAG, SELS[:, 0:128], GB16[:, :],
                          start=False, stop=True).then_inc(s_gmm, 1)
                for c in range(NCH):
                    lhsT = L3TD[t % 2][:, c * 128:(c + 1) * 128]
                    for nn in range(4):
                        te.matmul(PAUX[:, nn * 512:(nn + 1) * 512], lhsT,
                                  WTB[:, c * 2048 + nn * 512:
                                      c * 2048 + (nn + 1) * 512],
                                  start=(c == 0), stop=False)
                for nn in range(4):
                    te.matmul(PAUX[:, nn * 512:(nn + 1) * 512],
                              SEL4[:, nn * 128:(nn + 1) * 128], BROW4[:, :],
                              start=False, stop=False)
                te.wait_ge(s_xas, t + 1)
                te.transpose(PXAST, XAS, IDENT[:, :]).then_inc(s_xat, 1)
                te.wait_ge(s_xatev, t + 1)
                for nn in range(4):
                    ins = te.matmul(PAUX[:, nn * 512:(nn + 1) * 512], XAST,
                                    BBTB[:, nn * 512:(nn + 1) * 512],
                                    start=False, stop=True)
                ins.then_inc(s_ffnmm, 1)
                if t + 1 < NT:
                    l3_transposes(te, t + 1)

    with nc.Block("fin") as blk:
        @blk.sync
        def _(sy):
            sy.wait_ge(s_outp[0], 16 * 8)
            sy.wait_ge(s_outp[1], 16 * 8)

    return nc


_cached = {}


def _get_nc(debug=()):
    key = tuple(sorted(debug))
    if key not in _cached:
        _cached[key] = build_nc(debug=key)
    return _cached[key]


def make_in_maps(inputs):
    cs = _host_consts()
    W_q = np.asarray(inputs["L0_attn_W_q"], dtype=np.float32)
    W_kv = np.asarray(inputs["L0_attn_W_kv"], dtype=np.float32)
    W_o = np.asarray(inputs["L0_attn_W_o"], dtype=np.float32)
    W = np.asarray(inputs["L0_ffn_W"], dtype=np.float32)
    b = np.asarray(inputs["L0_ffn_b"], dtype=np.float32)
    lA = np.asarray(inputs["L0_ffn_lora_A"], dtype=np.float32)
    lB = np.asarray(inputs["L0_ffn_lora_B"], dtype=np.float32)
    gw = np.asarray(inputs["L0_ffn_gate_w"], dtype=np.float32)
    gb = np.asarray(inputs["L0_ffn_gate_b"], dtype=np.float32)
    x = np.asarray(inputs["L0_attn_x"], dtype=np.float32)

    def strips(Wm):
        return np.ascontiguousarray(
            Wm.T.reshape(NCH, 128, NH, 128).transpose(2, 1, 0, 3)
        ).reshape(NH, 128, D)

    wqs = strips(W_q)
    wkvs = strips(W_kv)
    wot = np.ascontiguousarray(W_o.T)
    wt = np.ascontiguousarray(W.T)
    A_allT = lA.reshape(128, D).T
    comb = np.concatenate([A_allT.reshape(NCH, 128, 128),
                           gw.T.reshape(NCH, 128, 8)], axis=2)
    agw = np.ascontiguousarray(comb.transpose(1, 0, 2)).reshape(128, NCH * 136)
    bbt = np.ascontiguousarray(lB.transpose(0, 2, 1)).reshape(128, D)
    brow4 = np.ascontiguousarray(b.reshape(4, 512))
    gb16 = np.zeros((16, 136), dtype=np.float32)
    gb16[0, 128:136] = gb

    shared = dict(wqs=wqs, wkvs=wkvs, wot=wot, wt=wt, agw=agw, bbt=bbt,
                  brow4=brow4, gb16=gb16,
                  cmat=cs["cmat"], msin=cs["msin"], ident=cs["ident"],
                  hblk=cs["hblk"], sels=cs["sels"], sel4=cs["sel4"])
    in_maps = []
    for core in range(NCORES):
        m = dict(shared)
        m["x"] = np.ascontiguousarray(x[core])
        in_maps.append(m)
    return in_maps


def kernel(**inputs):
    nc = _get_nc()
    in_maps = make_in_maps(inputs)
    res = run_bass_kernel_spmd(nc, in_maps, core_ids=list(range(NCORES)))
    out = np.stack([np.asarray(r["out"], dtype=np.float32)
                    for r in res.results], axis=0)
    return out


# version 27 (final)
# speedup vs baseline: 1.0096x; 1.0096x over previous
import numpy as np
import ml_dtypes
import concourse.bass as bass
import concourse.mybir as mybir
from concourse.bass_utils import run_bass_kernel_spmd

F32 = mybir.dt.float32
F32R = mybir.dt.float32r
BF16 = mybir.dt.bfloat16
AF = mybir.ActivationFunctionType
ALU = mybir.AluOpType
AXX = mybir.AxisListType.X

D = 2048
S = 2048
NT = 16
NH = 16
HD = 128
NCH = 16
EPS = 1e-5
NCORES = 8

BF = ml_dtypes.bfloat16


def _host_consts():
    j = np.arange(HD)
    ang = 2.0 * np.pi * (np.outer(j, j) % HD) / HD
    cmat = np.cos(ang).astype(np.float32).astype(BF)
    msin = (-np.sin(ang)).astype(np.float32).astype(BF)
    ident = np.eye(HD, dtype=BF)
    hblk = np.zeros((HD, NH * 16), dtype=BF)
    for h in range(NH):
        hblk[:, h * 16 + h] = 1.0
    sels = np.zeros((16, NH * HD), dtype=BF)
    for h in range(NH):
        sels[h, h * HD:(h + 1) * HD] = 1.0
    sel4 = np.zeros((4, 512), dtype=BF)
    for k in range(4):
        sel4[k, k * 128:(k + 1) * 128] = 1.0
    return dict(cmat=cmat, msin=msin, ident=ident, hblk=hblk, sels=sels,
                sel4=sel4)


def build_nc(debug=(), stop_after=None, cmode="full", dmode="full", dmode2="paux"):
    nc = bass.Bass("TRN2", target_bir_lowering=False,
                   dynamic_dma_scratch_size=16384)

    x_d = nc.dram_tensor("x", [S, D], F32, kind="ExternalInput")
    wqs_d = nc.dram_tensor("wqs", [NH, 128, D], F32, kind="ExternalInput")
    wkvs_d = nc.dram_tensor("wkvs", [NH, 128, D], F32, kind="ExternalInput")
    wot_d = nc.dram_tensor("wot", [D, D], F32, kind="ExternalInput")
    wt_d = nc.dram_tensor("wt", [D, D], F32, kind="ExternalInput")
    agw_d = nc.dram_tensor("agw", [128, NCH * 136], F32, kind="ExternalInput")
    bbt_d = nc.dram_tensor("bbt", [128, D], F32, kind="ExternalInput")
    brow4_d = nc.dram_tensor("brow4", [4, 512], F32, kind="ExternalInput")
    gb16_d = nc.dram_tensor("gb16", [16, 136], F32, kind="ExternalInput")
    cmat_d = nc.dram_tensor("cmat", [HD, HD], BF16, kind="ExternalInput")
    msin_d = nc.dram_tensor("msin", [HD, HD], BF16, kind="ExternalInput")
    ident_d = nc.dram_tensor("ident", [HD, HD], BF16, kind="ExternalInput")
    hblk_d = nc.dram_tensor("hblk", [HD, NH * 16], BF16, kind="ExternalInput")
    sels_d = nc.dram_tensor("sels", [16, NH * HD], BF16, kind="ExternalInput")
    sel4_d = nc.dram_tensor("sel4", [4, 512], BF16, kind="ExternalInput")

    x1s_d = nc.dram_tensor("x1s", [S, D], BF16, kind="Internal")
    wotb_d = nc.dram_tensor("wotb_s", [D, D], BF16, kind="Internal")
    wtb_d = nc.dram_tensor("wtb_s", [D, D], BF16, kind="Internal")
    out_d = nc.dram_tensor("out", [S, D], F32, kind="ExternalOutput")
    dbg_d = {}
    for name, shape, dt in [
        ("dbg_xct", [128, NCH * S], BF16), ("dbg_q", [128, NH * S], BF16),
        ("dbg_kv", [128, NH * S], BF16), ("dbg_invq", [16, S], BF16),
        ("dbg_invkv", [16, S], BF16), ("dbg_vh", [128, NH * S], BF16),
        ("dbg_x1", [S, D], BF16),
    ]:
        if name in debug:
            dbg_d[name] = nc.dram_tensor(name, shape, dt, kind="ExternalOutput")

    AR1 = nc.alloc_sbuf_tensor("AR1", [128, NH * S], BF16)
    AR2 = nc.alloc_sbuf_tensor("AR2", [128, NCH * S], BF16)
    AR3 = nc.alloc_sbuf_tensor("AR3", [128, NH * S], BF16)
    OV1 = nc.alloc_sbuf_tensor("OV1", [128, 6144], BF16)

    AR2_F = AR2.bitcast(F32)
    AR3_F = AR3.bitcast(F32)
    OV1_F = OV1.bitcast(F32)

    Q = AR1
    WTB = AR1
    XCT = AR2
    WOTB = AR2
    KV = AR3
    VH = AR3

    XRL = [AR3[:, 0:2048], AR3[:, 2048:4096]]
    XCC = [AR3[:, 4096:6144], AR3[:, 6144:8192]]

    STRIP = [OV1[:, 0:2048], OV1[:, 2048:4096]]
    SQ = [OV1[:, 4096:6144], OV1[:, 4096:6144]]
    QSTG = AR2_F[0:16, 2048:4096]
    KSTG = AR2_F[0:16, 0:2048]
    SELS = OV1[0:16, 0:2048]
    AGWB = OV1[:, 1024:3200]
    BBTB = OV1[:, 4096:6144]

    QN = AR2[:, 0:2048]
    KVN = AR2[:, 2048:4096]
    QF_RE = AR2[:, 4096:6144]
    QF_IM = AR2[:, 6144:8192]
    QV_RE = AR2[:, 8192:10240]
    QV_IM = AR2[:, 10240:12288]
    KVF_RE = AR2[:, 24576:26624]
    KVF_IM = AR2[:, 26624:28672]
    T1 = AR2[:, 20480:22528]
    T2 = AR2[:, 22528:24576]
    INVKV = AR2[0:16, 12288:14336]
    T1G = AR2[:, 28672:30720]
    T2G = AR2[:, 30720:32768]
    INVB = [OV1[:, 4096:5120], OV1[:, 5120:6144],
            AR2[:, 14336:15360], AR2[:, 15360:16384]]
    KVC_RE = AR2[:, 16384:18432]
    KVC_IM = AR2[:, 18432:20480]

    INVQ = OV1[0:16, 2048:4096]
    XRC = [AR1[:, 28672:30720], AR1[:, 30720:32768]]

    LN3D = [AR3[:, 0:2048], AR3[:, 26624:28672]]
    L3TD = [AR3[:, 2048:4096], AR3[:, 24576:26624]]
    XAS = AR3[:, 4096:4224]
    XAST = AR3[:, 4352:4480]
    OUTS_F = [AR3_F[:, 4096:6144], AR3_F[:, 6144:8192]]
    X1R = [AR3[:, 16384:18432], AR3[:, 18432:20480]]
    FFNT = [AR3[:, 20480:22528], AR3[:, 22528:24576]]

    CMAT = nc.alloc_sbuf_tensor("CMAT", [HD, HD], BF16)
    MSIN = nc.alloc_sbuf_tensor("MSIN", [HD, HD], BF16)
    IDENT = nc.alloc_sbuf_tensor("IDENT", [HD, HD], BF16)
    HBLK = nc.alloc_sbuf_tensor("HBLK", [HD, NH * 16], BF16)
    SEL4 = nc.alloc_sbuf_tensor("SEL4", [4, 512], BF16)
    BROW4 = nc.alloc_sbuf_tensor("BROW4", [4, 512], BF16)
    GB16 = nc.alloc_sbuf_tensor("GB16", [16, 136], BF16)
    STAT = nc.alloc_sbuf_tensor("STAT", [128, 24], F32)
    AGGR4 = nc.alloc_sbuf_tensor("AGGR4", [128, 4], F32)
    AGGR = AGGR4[:, 0:2]
    COLS = nc.alloc_sbuf_tensor("COLS", [128, 8], F32)
    LG = nc.alloc_sbuf_tensor("LG", [128, 8], F32)
    IS1 = nc.alloc_sbuf_tensor("IS1", [128, 8], F32)
    IS2 = nc.alloc_sbuf_tensor("IS2", [128, 8], F32)
    L2 = nc.alloc_sbuf_tensor("L2", [128, 8], F32)
    W8 = nc.alloc_sbuf_tensor("W8", [128, 8], F32)
    EPSC = AGGR4[:, 3:4]

    PMAIN = nc.alloc_psum_tensor("PMAIN", [128, 2048], F32)
    PAUX = nc.alloc_psum_tensor("PAUX", [128, 2048], F32)
    PMAIN_B = PMAIN.bitcast(BF16)
    PAUX_B = PAUX.bitcast(BF16)
    PTR = [PMAIN_B[:, 0:512], PMAIN_B[:, 1024:1536]]
    NORMS_Q = PAUX[0:16, :]
    NORMS_KV = PAUX[32:48, :]
    PXA = PMAIN[:, 1024:1152]
    PXAG = PMAIN[:, 1024:1160]
    PLG = PMAIN[:, 1152:1160]
    PL3T = [PMAIN_B[:, 0:512], PMAIN_B[:, 1024:1536]]
    PXAST = PMAIN_B[:, 2560:2688]

    sm = lambda n: nc.alloc_semaphore(n)
    s_const = sm("s_const"); s_const2 = sm("s_const2")
    s_xp = [sm("s_xp0"), sm("s_xp1")]
    s_m = sm("s_m"); s_ctr = sm("s_ctr")
    s_tr = sm("s_tr"); s_trev = sm("s_trev")
    s_sp = [sm("s_sp0"), sm("s_sp1")]
    s_qmm = sm("s_qmm"); s_qev = sm("s_qev"); s_qevl = sm("s_qevl")
    s_sq = sm("s_sq")
    s_normmm = sm("s_normmm")
    s_sels = sm("s_sels"); s_invrdy = sm("s_invrdy")
    s_fa = sm("s_fa"); s_fb = sm("s_fb"); s_scand = sm("s_scand")
    s_dft = sm("s_dft"); s_dftev = sm("s_dftev")
    s_cm = sm("s_cm"); s_ifft = sm("s_ifft"); s_vhev = sm("s_vhev")
    s_wot = sm("s_wot"); s_attn = sm("s_attn")
    s_wog = [sm("s_wog0"), sm("s_wog1"), sm("s_wog2"), sm("s_wog3")]
    s_stat = sm("s_stat"); s_sqrt = sm("s_sqrt")
    s_xrp = [sm("s_xrp0"), sm("s_xrp1")]
    s_res = sm("s_res")
    s_x1sp = [sm("s_x1sp0"), sm("s_x1sp1")]
    s_wt = sm("s_wt"); s_dsm = sm("s_dsm")
    s_x1rp = [sm("s_x1rp0"), sm("s_x1rp1")]
    s_l3 = sm("s_l3"); s_l3t = sm("s_l3t"); s_l3tev = sm("s_l3tev")
    s_gmm = sm("s_gmm"); s_xamm = sm("s_xamm"); s_gprod = sm("s_gprod")
    s_qkm = sm("s_qkm"); s_pcast = sm("s_pcast"); s_pcast2 = sm("s_pcast2")
    s_wogp = sm("s_wogp")
    s_gate1 = sm("s_gate1"); s_gate2 = sm("s_gate2")
    s_xas = sm("s_xas"); s_xat = sm("s_xat"); s_xatev = sm("s_xatev")
    s_ffnmm = sm("s_ffnmm")
    s_stat4 = sm("s_stat4"); s_sqrt4 = sm("s_sqrt4"); s_ffev = sm("s_ffev")
    s_st3 = sm("s_st3"); s_sq3 = sm("s_sq3")
    s_ln4ev = sm("s_ln4ev"); s_res4 = sm("s_res4")
    s_outp = [sm("s_outp0"), sm("s_outp1")]
    s_dbg = sm("s_dbg")

    with nc.Block("ph1") as blk:
        @blk.sync
        def _(sy):
            sy.dma_start(CMAT[:, :], cmat_d.ap()).then_inc(s_const, 16)
            sy.dma_start(MSIN[:, :], msin_d.ap()).then_inc(s_const, 16)
            sy.dma_start(IDENT[:, :], ident_d.ap()).then_inc(s_const, 16)
            sy.dma_start(HBLK[:, :], hblk_d.ap()).then_inc(s_const, 16)
            sy.dma_start(SEL4[:, :], sel4_d.ap()).then_inc(s_const, 16)

        @blk.gpsimd
        def _(g):
            g.memset(COLS[:, 5:6], 0.0)
            g.memset(EPSC[:, :], EPS)
            g.dma_start(BROW4[:, :], brow4_d.ap()).then_inc(s_const2, 16)
            g.dma_start(GB16[:, :], gb16_d.ap()).then_inc(s_const2, 16)
            for t in range(NT):
                if t >= 2:
                    g.wait_ge(s_m, t - 1)
                g.dma_start(XRL[t % 2], x_d.ap()[t * 128:(t + 1) * 128, :]
                            ).then_inc(s_xp[t % 2], 16)
            for h in (0, 1):
                g.dma_start(STRIP[h], wqs_d.ap()[h]).then_inc(s_sp[h], 16)

        @blk.scalar
        def _(a):
            for t in range(NT):
                a.wait_ge(s_xp[t % 2], 16 * (t // 2 + 1))
                if t >= 1:
                    a.wait_ge(s_ctr, t)
                if t >= 2:
                    a.wait_ge(s_tr, 4 * (t - 1))
                a.activation(XCC[t % 2], XRL[t % 2], AF.Identity,
                             accum_out=COLS[:, 3:4]).then_inc(s_m, 1)

        @blk.vector
        def _(v):
            for t in range(NT):
                v.wait_ge(s_m, t + 1)
                v.tensor_scalar_mul(COLS[:, 4:5], COLS[:, 3:4], 1.0 / D)
                v.drain()
                v.tensor_scalar(XCC[t % 2], XCC[t % 2], COLS[:, 4:5], None,
                                ALU.subtract).then_inc(s_ctr, 1)
                for grp in range(4):
                    gi = t * 4 + grp
                    v.wait_ge(s_tr, gi + 1)
                    src = PTR[gi % 2].rearrange("p (c n) -> p c n", c=4)
                    dst = XCT.rearrange("p (c n) -> p c n", c=NCH)[
                        :, 4 * grp:4 * grp + 4, t * 128:(t + 1) * 128]
                    v.tensor_copy(dst, src).then_inc(s_trev, 1)

        @blk.tensor
        def _(te):
            te.wait_ge(s_const, 16 * 5)
            for t in range(NT):
                te.wait_ge(s_ctr, t + 1)
                for grp in range(4):
                    gi = t * 4 + grp
                    if gi >= 2:
                        te.wait_ge(s_trev, gi - 1)
                    for cc in range(4):
                        c = grp * 4 + cc
                        ins = te.transpose(
                            PTR[gi % 2][:, cc * 128:(cc + 1) * 128],
                            XCC[t % 2][:, c * 128:(c + 1) * 128],
                            IDENT[:, :])
                        if cc == 3:
                            ins.then_inc(s_tr, 1)

    dbgn = [0]

    def _fin_dummy():
        with nc.Block("findummy") as blk:
            @blk.gpsimd
            def _(g):
                g.dma_start(out_d.ap()[0:128, 0:24], STAT[:, :]).then_inc(s_dbg, 16)
                g.wait_ge(s_dbg, dbgn[0] + 16)

    def dbg_dump(g, name, srcap):
        g.dma_start(dbg_d[name].ap(), srcap).then_inc(s_dbg, 16)
        dbgn[0] += 16

    if stop_after == "ph1":
        _fin_dummy()
        return nc

    if "dbg_xct" in dbg_d:
        with nc.Block("dbgxct") as blk:
            @blk.gpsimd
            def _(g):
                dbg_dump(g, "dbg_xct", XCT[:, :])
                g.wait_ge(s_dbg, dbgn[0])

    def b1_phase(blk, strip_dram, DEST, NORMS, base, prefetch_dram=None):

        @blk.gpsimd
        def _(g):
            for h in range(NH):
                if h < 2 and base["sp"] == 0:
                    continue
                if h == 0 and base["sp"] != 0:
                    continue
                if h >= 2:
                    g.wait_ge(s_qmm, base["qmm"] + h - 1)
                g.dma_start(STRIP[h % 2], strip_dram.ap()[h]
                            ).then_inc(s_sp[h % 2], 16)
            if prefetch_dram is not None:
                g.wait_ge(s_qmm, base["qmm"] + NH - 1)
                g.dma_start(STRIP[0], prefetch_dram.ap()[0]
                            ).then_inc(s_sp[0], 16)

        @blk.tensor
        def _(te):
            for h in range(NH):
                te.wait_ge(s_sp[h % 2], 16 * (base["sp"] + h // 2 + 1))
                if h >= 1:
                    te.wait_ge(s_qevl, base["qev"] + h)
                for c in range(NCH):
                    lhsT = STRIP[h % 2][:, c * 128:(c + 1) * 128]
                    for nn in range(4):
                        if h >= 1 and c == 0 and nn == 2:
                            te.wait_ge(s_qev, base["qev"] + h)
                        ins = te.matmul(PMAIN[:, nn * 512:(nn + 1) * 512], lhsT,
                                        XCT[:, c * 2048 + nn * 512:
                                            c * 2048 + (nn + 1) * 512],
                                        start=(c == 0), stop=(c == NCH - 1))
                        if c == NCH - 1 and nn == 3:
                            ins.then_inc(s_qmm, 1)
                if h >= 1:
                    te.wait_ge(s_sq, base["sq"] + h)
                    lhsT = HBLK[:, (h - 1) * 16:h * 16]
                    for nn in range(4):
                        ins = te.matmul(NORMS[:, nn * 512:(nn + 1) * 512], lhsT,
                                        SQ[(h - 1) % 2][:, nn * 512:(nn + 1) * 512],
                                        start=(h - 1 == 0), stop=False)
                    ins.then_inc(s_normmm, 1)
            te.wait_ge(s_sq, base["sq"] + NH)
            lhsT = HBLK[:, (NH - 1) * 16:NH * 16]
            for nn in range(4):
                ins = te.matmul(NORMS[:, nn * 512:(nn + 1) * 512], lhsT,
                                SQ[(NH - 1) % 2][:, nn * 512:(nn + 1) * 512],
                                start=False, stop=True)
            ins.then_inc(s_normmm, 1)

        @blk.scalar
        def _(a):
            for h in range(NH):
                a.wait_ge(s_qmm, base["qmm"] + h + 1)
                a.activation(DEST[:, h * 2048:h * 2048 + 1024],
                             PMAIN[:, 0:1024],
                             AF.Identity).then_inc(s_qevl, 1)
                a.activation(DEST[:, h * 2048 + 1024:(h + 1) * 2048],
                             PMAIN[:, 1024:2048],
                             AF.Identity).then_inc(s_qev, 1)

        @blk.vector
        def _(v):
            for h in range(NH):
                v.wait_ge(s_qev, base["qev"] + h + 1)
                if h >= 1:
                    v.wait_ge(s_normmm, base["norm"] + h)
                v.tensor_mul(SQ[h % 2], DEST[:, h * 2048:(h + 1) * 2048],
                             DEST[:, h * 2048:(h + 1) * 2048]).then_inc(s_sq, 1)

    with nc.Block("b1q") as blk:
        b1_phase(blk, wqs_d, Q, NORMS_Q,
                 dict(sp=0, qev=0, sq=0, qmm=0, norm=0), prefetch_dram=wkvs_d)

    if stop_after == "b1q":
        _fin_dummy()
        return nc

    with nc.Block("b1kv") as blk:
        b1_phase(blk, wkvs_d, KV, NORMS_KV,
                 dict(sp=NH // 2, qev=NH, sq=NH, qmm=NH, norm=NH))

    if stop_after == "b1kv":
        _fin_dummy()
        return nc

    if "dbg_q" in dbg_d:
        with nc.Block("dbgq") as blk:
            @blk.gpsimd
            def _(g):
                dbg_dump(g, "dbg_q", Q[:, :])
                dbg_dump(g, "dbg_kv", KV[:, :])
                g.wait_ge(s_dbg, dbgn[0])

    with nc.Block("b2") as blk:
        @blk.sync
        def _(sy):
            sy.dma_start(SELS, sels_d.ap()).then_inc(s_sels, 16)

        @blk.gpsimd
        def _(g):
            for c in range(NCH):
                g.dma_start(wotb_d.ap()[c * 128:(c + 1) * 128, :],
                            wot_d.ap()[c * 128:(c + 1) * 128, :]
                            ).then_inc(s_pcast, 16)
            for c in range(NCH):
                g.dma_start(wtb_d.ap()[c * 128:(c + 1) * 128, :],
                            wt_d.ap()[c * 128:(c + 1) * 128, :]
                            ).then_inc(s_pcast2, 16)
            g.wait_ge(s_pcast, 16 * NCH)
            for c in (NCH - 2, NCH - 1):
                g.dma_start(WOTB[:, c * 2048:(c + 1) * 2048],
                            wotb_d.ap()[c * 128:(c + 1) * 128, :]
                            ).then_inc(s_wogp, 16)

        @blk.scalar
        def _(a):
            a.activation(QSTG, NORMS_Q[:, :], AF.Ln, scale=float(128 ** 3))
            a.activation(KSTG, NORMS_KV[:, :], AF.Ln, scale=128.0)
            a.drain()
            a.activation(INVQ, QSTG, AF.Exp, scale=-0.5).then_inc(s_invrdy, 1)
            a.activation(INVKV, KSTG, AF.Exp, scale=-0.5).then_inc(s_invrdy, 1)

            def vh_evac(a, hh):
                a.wait_ge(s_ifft, hh + 1)
                a.activation(VH[:, hh * 2048:(hh + 1) * 2048], PMAIN[:, :],
                             AF.Identity).then_inc(s_vhev, 1)

            sf = [s_fa, s_fb]
            for h in range(NH):
                for j in range(12):
                    p = j % 2
                    a.wait_ge(s_dft, 12 * h + j + 1)
                    if j < 4:
                        if h >= 1:
                            a.wait_ge(s_qkm, 4 * (h - 1) + j + 1)
                        a.activation(INVB[j],
                                     PAUX[:, p * 1024:(p + 1) * 1024],
                                     AF.Identity).then_inc(sf[p], 1)
                        continue
                    if j == 4 and h >= 1:
                        a.wait_ge(s_scand, h)
                    if j == 8 and h >= 1:
                        a.wait_ge(s_cm, h)
                    dst_t = [KVF_RE, KVF_RE, KVF_IM, KVF_IM,
                             QF_RE, QF_RE, QF_IM, QF_IM][j - 4]
                    a.activation(dst_t[:, p * 1024:(p + 1) * 1024],
                                 PAUX[:, p * 1024:(p + 1) * 1024],
                                 AF.Identity).then_inc(sf[p], 1)
                if h >= 1:
                    vh_evac(a, h - 1)
            vh_evac(a, NH - 1)

        @blk.tensor
        def _(te):
            te.wait_ge(s_sels, 16)
            te.wait_ge(s_invrdy, 2)

            def ifft(te, hh):
                te.wait_ge(s_cm, hh + 1)
                te.wait_ge(s_vhev, hh)
                for nn in range(4):
                    te.matmul(PMAIN[:, nn * 512:(nn + 1) * 512], CMAT[:, :],
                              QV_RE[:, nn * 512:(nn + 1) * 512],
                              start=True, stop=False)
                for nn in range(4):
                    ins = te.matmul(PMAIN[:, nn * 512:(nn + 1) * 512],
                                    MSIN[:, :],
                                    QV_IM[:, nn * 512:(nn + 1) * 512],
                                    start=False, stop=True)
                ins.then_inc(s_ifft, 1)

            sf = [s_fa, s_fb]
            for h in range(NH):
                selh = SELS[:, h * 128:(h + 1) * 128]
                for j in range(12):
                    p = j % 2
                    if 12 * h + j >= 2:
                        te.wait_ge(sf[p], 6 * h + j // 2)
                    if j in (4, 5):
                        te.wait_ge(s_qkm, 4 * h + 3 + (j - 4))
                    if j in (8, 9):
                        te.wait_ge(s_qkm, 4 * h + 1 + (j - 8))
                    if j < 4:
                        lhsT = selh
                        rhs_t = INVQ if j < 2 else INVKV
                        rhs = rhs_t[:, p * 1024:(p + 1) * 1024]
                    else:
                        lhsT = CMAT[:, :] if (j // 2) % 2 == 0 else MSIN[:, :]
                        srcb = KVN if j < 8 else QN
                        rhs = srcb[:, p * 1024:(p + 1) * 1024]
                    for nn in range(2):
                        o = p * 1024 + nn * 512
                        ins = te.matmul(PAUX[:, o:o + 512], lhsT,
                                        rhs[:, nn * 512:(nn + 1) * 512],
                                        start=True, stop=True)
                    ins.then_inc(s_dft, 1)
                if h >= 1:
                    ifft(te, h - 1)
            ifft(te, NH - 1)

        @blk.vector
        def _(v):
            zb = COLS[:, 5:6].broadcast_to([128, 2048])
            sf = [s_fa, s_fb]

            def midsec_a(v, h):
                v.wait_ge(s_fa, 6 * h + 4)
                v.wait_ge(s_fb, 6 * h + 4)
                v.tensor_tensor_scan(KVC_RE, KVF_RE, zb, 0.0, ALU.add, ALU.add)
                v.tensor_tensor_scan(KVC_IM, KVF_IM, zb, 0.0, ALU.add, ALU.add
                                     ).then_inc(s_scand, 1)
                v.drain()

            def midsec_b(v, h):
                v.wait_ge(s_fa, 6 * h + 6)
                v.wait_ge(s_fb, 6 * h + 6)
                v.tensor_mul(T1, QF_RE, KVC_RE)
                v.tensor_mul(T2, QF_IM, KVC_IM)
                v.drain()
                v.wait_ge(s_ifft, h)
                v.tensor_sub(QV_RE, T1, T2)
                v.drain()
                v.tensor_mul(T1, QF_RE, KVC_IM)
                v.tensor_mul(T2, QF_IM, KVC_RE)
                v.drain()
                v.tensor_add(QV_IM, T1, T2).then_inc(s_cm, 1)
                v.drain()

            for h in range(NH):
                if h >= 1:
                    midsec_a(v, h - 1)
                for j in range(4):
                    p = j % 2
                    v.wait_ge(sf[p], 6 * h + j // 2 + 1)
                    dst_t = QN if j < 2 else KVN
                    src_t = Q if j < 2 else KV
                    v.tensor_mul(dst_t[:, p * 1024:(p + 1) * 1024],
                                 src_t[:, h * 2048 + p * 1024:
                                       h * 2048 + (p + 1) * 1024],
                                 INVB[j]).then_inc(s_qkm, 1)
                if h >= 1:
                    midsec_b(v, h - 1)
            midsec_a(v, NH - 1)
            midsec_b(v, NH - 1)



    if stop_after == "b2":
        _fin_dummy()
        return nc

    if "dbg_vh" in dbg_d or "dbg_invq" in dbg_d:
        with nc.Block("dbgvh") as blk:
            @blk.gpsimd
            def _(g):
                if "dbg_vh" in dbg_d:
                    dbg_dump(g, "dbg_vh", VH[:, :])
                if "dbg_invq" in dbg_d:
                    dbg_dump(g, "dbg_invq", INVQ)
                    dbg_dump(g, "dbg_invkv", INVKV)
                g.wait_ge(s_dbg, dbgn[0])

    with nc.Block("c") as blk:
        @blk.gpsimd
        def _(g):
            for c in range(NCH - 2):
                g.dma_start(WOTB[:, c * 2048:(c + 1) * 2048],
                            wotb_d.ap()[c * 128:(c + 1) * 128, :]
                            ).then_inc(s_wog[c // 4], 16)
            g.dma_start(XRC[0], x_d.ap()[0:128, :]).then_inc(s_xrp[0], 16)
            g.dma_start(XRC[1], x_d.ap()[128:256, :]).then_inc(s_xrp[1], 16)
            g.wait_ge(s_pcast2, 16 * NCH)
            for t in range(NT):
                g.wait_ge(s_res, t + 1)
                g.dma_start(x1s_d.ap()[t * 128:(t + 1) * 128, :], XRC[t % 2]
                            ).then_inc(s_x1sp[t % 2], 16)
                if t + 2 < NT:
                    g.wait_ge(s_x1sp[t % 2], 16 * (t // 2 + 1))
                    g.dma_start(XRC[t % 2],
                                x_d.ap()[(t + 2) * 128:(t + 3) * 128, :]
                                ).then_inc(s_xrp[t % 2], 16)
                if t < NCH - 2:
                    g.dma_start(WTB[:, t * 2048:(t + 1) * 2048],
                                wtb_d.ap()[t * 128:(t + 1) * 128, :]
                                ).then_inc(s_wt, 16)
                if t == NCH - 2:
                    g.dma_start(AGWB, agw_d.ap()).then_inc(s_dsm, 16)
            g.wait_ge(s_attn, NT)
            for t in (0, 1):
                g.wait_ge(s_x1sp[t % 2], 16)
                g.dma_start(X1R[t], x1s_d.ap()[t * 128:(t + 1) * 128, :]
                            ).then_inc(s_x1rp[t], 16)

        @blk.tensor
        def _(te):
            for t in range(NT):
                pm = PMAIN if t % 2 == 0 else PAUX
                if t >= 2:
                    te.wait_ge(s_res, t - 1)
                order = ([NCH - 2, NCH - 1] + list(range(NCH - 2))) if t == 0 \
                    else list(range(NCH))
                for ci, c in enumerate(order):
                    if t == 0:
                        if ci == 0:
                            te.wait_ge(s_wogp, 32)
                        elif c % 4 == 0 and c < NCH - 4:
                            te.wait_ge(s_wog[c // 4], 64)
                        elif c == NCH - 4:
                            te.wait_ge(s_wog[3], 32)
                    lhsT = VH[:, c * 2048 + t * 128: c * 2048 + (t + 1) * 128]
                    for nn in range(4):
                        ins = te.matmul(pm[:, nn * 512:(nn + 1) * 512], lhsT,
                                        WOTB[:, c * 2048 + nn * 512:
                                             c * 2048 + (nn + 1) * 512],
                                        start=(ci == 0), stop=(ci == NCH - 1))
                ins.then_inc(s_attn, 1)

        @blk.vector
        def _(v):
            for t in range(NT):
                pm = PMAIN if t % 2 == 0 else PAUX
                v.wait_ge(s_attn, t + 1)
                if cmode == "nostat":
                    v.wait_ge(s_xrp[t % 2], 16 * (t // 2 + 1))
                    v.tensor_scalar(XRC[t % 2], XRC[t % 2], COLS[:, 5:6], None,
                                    ALU.add)
                    v.drain()
                    v.scalar_tensor_tensor(XRC[t % 2], pm[:, :], COLS[:, 5:6],
                                           XRC[t % 2], ALU.mult, ALU.add
                                           ).then_inc(s_res, 1)
                    continue
                if cmode == "min":
                    v.wait_ge(s_xrp[t % 2], 16 * (t // 2 + 1))
                    v.tensor_scalar(XRC[t % 2], XRC[t % 2], COLS[:, 5:6], None,
                                    ALU.add).then_inc(s_res, 1)
                    continue
                if t >= 1:
                    v.wait_ge(s_sqrt, t)
                for ch in range(4):
                    v.bn_stats(STAT[:, 6 * ch:6 * ch + 6],
                               pm[:, 512 * ch:512 * (ch + 1)])
                v.drain()
                v.bn_aggr(AGGR, STAT[:, :]).then_inc(s_stat, 1)
                v.wait_ge(s_sqrt, t + 1)
                v.reciprocal(COLS[:, 1:2], COLS[:, 0:1])
                v.drain()
                v.scalar_tensor_tensor(COLS[:, 2:3], AGGR4[:, 0:1], -1.0,
                                       COLS[:, 1:2], ALU.mult, ALU.mult)
                v.drain()
                v.wait_ge(s_xrp[t % 2], 16 * (t // 2 + 1))
                v.tensor_scalar(XRC[t % 2], XRC[t % 2], COLS[:, 2:3], None,
                                ALU.add)
                v.drain()
                if cmode == "nopsum":
                    v.tensor_scalar(XRC[t % 2], XRC[t % 2], COLS[:, 1:2], None,
                                    ALU.mult).then_inc(s_res, 1)
                else:
                    v.scalar_tensor_tensor(XRC[t % 2], pm[:, :], COLS[:, 1:2],
                                           XRC[t % 2], ALU.mult, ALU.add
                                           ).then_inc(s_res, 1)

        if cmode not in ("nostat", "min"):
            @blk.scalar
            def _(a):
                for t in range(NT):
                    a.wait_ge(s_stat, t + 1)
                    if t >= 1:
                        a.wait_ge(s_res, t)
                    a.activation(COLS[:, 0:1], AGGR4[:, 1:2], AF.Sqrt,
                                 bias=EPSC[:, :]).then_inc(s_sqrt, 1)

    if stop_after == "c":
        _fin_dummy()
        return nc

    if "dbg_x1" in dbg_d:
        with nc.Block("dbgx1") as blk:
            @blk.gpsimd
            def _(g):
                g.wait_ge(s_x1sp[0], 16 * 8)
                g.wait_ge(s_x1sp[1], 16 * 8)
                dbg_dump(g, "dbg_x1", x1s_d.ap())
                g.wait_ge(s_dbg, dbgn[0])

    with nc.Block("d") as blk:
        IS2v = LG
        LN4C = IS2

        @blk.gpsimd
        def _(g):
            g.wait_ge(s_x1sp[0], 16 * 8)
            g.wait_ge(s_x1sp[1], 16 * 8)
            for c in (NCH - 2, NCH - 1):
                g.dma_start(WTB[:, c * 2048:(c + 1) * 2048],
                            wtb_d.ap()[c * 128:(c + 1) * 128, :]
                            ).then_inc(s_wt, 16)
            g.dma_start(BBTB, bbt_d.ap()).then_inc(s_dsm, 16)
            for t in range(NT):
                if t >= 2:
                    g.wait_ge(s_res4, t - 1)
                    g.dma_start(X1R[t % 2],
                                x1s_d.ap()[t * 128:(t + 1) * 128, :]
                                ).then_inc(s_x1rp[t % 2], 16)
                if t >= 1:
                    g.wait_ge(s_res4, t)
                    g.dma_start(out_d.ap()[(t - 1) * 128:t * 128, :],
                                OUTS_F[(t - 1) % 2]).then_inc(s_outp[(t - 1) % 2], 16)
            g.wait_ge(s_res4, NT)
            g.dma_start(out_d.ap()[(NT - 1) * 128:NT * 128, :],
                        OUTS_F[(NT - 1) % 2]).then_inc(s_outp[(NT - 1) % 2], 16)

        @blk.vector
        def _(v):
            def gates_head(v, tt):
                v.wait_ge(s_gmm, tt + 1)
                v.tensor_copy(LG[:, :], PLG)
                v.drain()
                v.reduce_max(COLS[:, 5:6], LG[:, :], axis=AXX)
                v.drain()
                v.tensor_scalar(IS1[:, :], LG[:, :], COLS[:, 5:6], None,
                                ALU.is_ge)
                v.drain()
                v.scalar_tensor_tensor(L2[:, :], IS1[:, :], -1e30, LG[:, :],
                                       ALU.mult, ALU.add)
                v.drain()
                v.reduce_max(COLS[:, 6:7], L2[:, :], axis=AXX)
                v.drain()
                v.tensor_sub(COLS[:, 7:8], COLS[:, 6:7], COLS[:, 5:6]
                             ).then_inc(s_gate1, 1)
                v.wait_ge(s_gate2, tt + 1)
                v.tensor_scalar(COLS[:, 4:5], COLS[:, 3:4], -1.0, -1.0,
                                ALU.mult, ALU.subtract)
                v.drain()
                v.tensor_scalar(W8[:, :], IS1[:, :], COLS[:, 4:5], None,
                                ALU.mult)
                v.drain()
                v.tensor_scalar(IS2v[:, :], L2[:, :], COLS[:, 6:7], None,
                                ALU.is_ge)
                v.drain()
                v.scalar_tensor_tensor(W8[:, :], IS2v[:, :], COLS[:, 3:4],
                                       W8[:, :], ALU.mult, ALU.add)
                v.drain()
                v.tensor_mul(XAS.rearrange("p (a b) -> p a b", a=8),
                             PXA.rearrange("p (a b) -> p a b", a=8),
                             W8[:, :, None].broadcast_to([128, 8, 16])
                             ).then_inc(s_xas, 1)

            def xast_copy(v, tt):
                v.wait_ge(s_xat, tt + 1)
                v.tensor_copy(XAST, PXAST).then_inc(s_xatev, 1)

            def ln4_res(v, tt):
                v.wait_ge(s_ffev, tt + 1)
                if tt >= 1:
                    v.wait_ge(s_sqrt4, tt)
                for ch in range(4):
                    v.bn_stats(STAT[:, 6 * ch:6 * ch + 6],
                               FFNT[tt % 2][:, 512 * ch:512 * (ch + 1)])
                v.drain()
                v.bn_aggr(LN4C[:, 0:2], STAT[:, :]).then_inc(s_stat4, 1)
                v.wait_ge(s_sqrt4, tt + 1)
                v.reciprocal(LN4C[:, 3:4], LN4C[:, 2:3])
                v.drain()
                v.scalar_tensor_tensor(LN4C[:, 4:5], LN4C[:, 0:1], -1.0,
                                       LN4C[:, 3:4], ALU.mult, ALU.mult)
                v.drain()
                v.tensor_scalar(X1R[tt % 2], X1R[tt % 2], LN4C[:, 4:5], None,
                                ALU.add)
                v.drain()
                if tt >= 2:
                    v.wait_ge(s_outp[tt % 2], 16 * (tt // 2))
                v.scalar_tensor_tensor(OUTS_F[tt % 2], FFNT[tt % 2],
                                       LN4C[:, 3:4], X1R[tt % 2],
                                       ALU.mult, ALU.add).then_inc(s_res4, 1)

            for t in range(NT):
                if t >= 1:
                    gates_head(v, t - 1)
                v.wait_ge(s_x1rp[t % 2], 16 * (t // 2 + 1))
                if t >= 1:
                    v.wait_ge(s_sq3, t)
                for ch in range(4):
                    v.bn_stats(STAT[:, 6 * ch:6 * ch + 6],
                               X1R[t % 2][:, 512 * ch:512 * (ch + 1)])
                v.drain()
                v.bn_aggr(AGGR, STAT[:, :]).then_inc(s_st3, 1)
                v.wait_ge(s_sq3, t + 1)
                if t >= 1:
                    v.wait_ge(s_l3, 2 * t)
                v.reciprocal(COLS[:, 1:2], COLS[:, 0:1])
                v.drain()
                v.scalar_tensor_tensor(COLS[:, 2:3], AGGR4[:, 0:1], -1.0,
                                       COLS[:, 1:2], ALU.mult, ALU.mult
                                       ).then_inc(s_l3, 1)
                if t >= 1:
                    xast_copy(v, t - 1)
                if t >= 2:
                    v.wait_ge(s_ffnmm, t - 1)
                for grp in range(4):
                    v.wait_ge(s_l3t, 4 * t + grp + 1)
                    v.tensor_copy(L3TD[t % 2][:, grp * 512:(grp + 1) * 512],
                                  PL3T[grp % 2]).then_inc(s_l3tev, 1)
                if t >= 1:
                    ln4_res(v, t - 1)
            gates_head(v, NT - 1)
            xast_copy(v, NT - 1)
            ln4_res(v, NT - 1)

        @blk.scalar
        def _(a):
            for t in range(NT):
                if t >= 1:
                    tt = t - 1
                    a.wait_ge(s_gate1, tt + 1)
                    a.wait_ge(s_xas, tt)
                    a.activation(COLS[:, 3:4], COLS[:, 7:8], AF.Sigmoid
                                 ).then_inc(s_gate2, 1)
                a.wait_ge(s_st3, t + 1)
                a.activation(COLS[:, 0:1], AGGR4[:, 1:2], AF.Sqrt,
                             bias=EPSC).then_inc(s_sq3, 1)
                a.wait_ge(s_l3, 2 * t + 1)
                if t >= 2:
                    a.wait_ge(s_l3t, 4 * t - 4)
                a.activation(LN3D[t % 2], X1R[t % 2], AF.Identity,
                             scale=COLS[:, 1:2],
                             bias=COLS[:, 2:3]).then_inc(s_l3, 1)
                if t >= 1:
                    tt = t - 1
                    a.wait_ge(s_ffnmm, tt + 1)
                    if tt >= 2:
                        a.wait_ge(s_res4, tt - 1)
                    a.activation(FFNT[tt % 2], PAUX[:, :], AF.Identity
                                 ).then_inc(s_ffev, 1)
                    a.wait_ge(s_stat4, tt + 1)
                    a.activation(LN4C[:, 2:3], LN4C[:, 1:2], AF.Sqrt,
                                 bias=EPSC).then_inc(s_sqrt4, 1)
            tt = NT - 1
            a.wait_ge(s_gate1, tt + 1)
            a.wait_ge(s_xas, tt)
            a.activation(COLS[:, 3:4], COLS[:, 7:8], AF.Sigmoid
                         ).then_inc(s_gate2, 1)
            a.wait_ge(s_ffnmm, tt + 1)
            a.wait_ge(s_res4, tt - 1)
            a.activation(FFNT[tt % 2], PAUX[:, :], AF.Identity
                         ).then_inc(s_ffev, 1)
            a.wait_ge(s_stat4, tt + 1)
            a.activation(LN4C[:, 2:3], LN4C[:, 1:2], AF.Sqrt,
                         bias=EPSC).then_inc(s_sqrt4, 1)

        @blk.tensor
        def _(te):
            te.wait_ge(s_dsm, 32)
            te.wait_ge(s_const2, 32)
            te.wait_ge(s_wt, 16 * NCH)
            te.wait_ge(s_sels, 16)
            def l3_transposes(te, tt):
                te.wait_ge(s_l3, 2 * tt + 2)
                for grp in range(4):
                    if 4 * tt + grp >= 2:
                        te.wait_ge(s_l3tev, 4 * tt + grp - 1)
                    for cc in range(4):
                        c = grp * 4 + cc
                        ins = te.transpose(
                            PL3T[grp % 2][:, cc * 128:(cc + 1) * 128],
                            LN3D[tt % 2][:, c * 128:(c + 1) * 128],
                            IDENT[:, :])
                    ins.then_inc(s_l3t, 1)

            l3_transposes(te, 0)
            for t in range(NT):
                if t >= 1:
                    te.wait_ge(s_ffev, t)
                    te.wait_ge(s_xas, t)
                te.wait_ge(s_l3tev, 4 * t + 4)
                for c in range(NCH):
                    lhsT = L3TD[t % 2][:, c * 128:(c + 1) * 128]
                    te.matmul(PXAG, lhsT, AGWB[:, c * 136:(c + 1) * 136],
                              start=(c == 0), stop=False)
                te.matmul(PXAG, SELS[:, 0:128], GB16[:, :],
                          start=False, stop=True).then_inc(s_gmm, 1)
                for c in range(NCH):
                    lhsT = L3TD[t % 2][:, c * 128:(c + 1) * 128]
                    for nn in range(4):
                        te.matmul(PAUX[:, nn * 512:(nn + 1) * 512], lhsT,
                                  WTB[:, c * 2048 + nn * 512:
                                      c * 2048 + (nn + 1) * 512],
                                  start=(c == 0), stop=False)
                for nn in range(4):
                    te.matmul(PAUX[:, nn * 512:(nn + 1) * 512],
                              SEL4[:, nn * 128:(nn + 1) * 128], BROW4[:, :],
                              start=False, stop=False)
                te.wait_ge(s_xas, t + 1)
                te.transpose(PXAST, XAS, IDENT[:, :]).then_inc(s_xat, 1)
                te.wait_ge(s_xatev, t + 1)
                for nn in range(4):
                    ins = te.matmul(PAUX[:, nn * 512:(nn + 1) * 512], XAST,
                                    BBTB[:, nn * 512:(nn + 1) * 512],
                                    start=False, stop=True)
                ins.then_inc(s_ffnmm, 1)
                if t + 1 < NT:
                    l3_transposes(te, t + 1)

    with nc.Block("fin") as blk:
        @blk.sync
        def _(sy):
            sy.wait_ge(s_outp[0], 16 * 8)
            sy.wait_ge(s_outp[1], 16 * 8)

    return nc


_cached = {}


def _get_nc(debug=()):
    key = tuple(sorted(debug))
    if key not in _cached:
        _cached[key] = build_nc(debug=key)
    return _cached[key]


def make_in_maps(inputs):
    cs = _host_consts()
    W_q = np.asarray(inputs["L0_attn_W_q"], dtype=np.float32)
    W_kv = np.asarray(inputs["L0_attn_W_kv"], dtype=np.float32)
    W_o = np.asarray(inputs["L0_attn_W_o"], dtype=np.float32)
    W = np.asarray(inputs["L0_ffn_W"], dtype=np.float32)
    b = np.asarray(inputs["L0_ffn_b"], dtype=np.float32)
    lA = np.asarray(inputs["L0_ffn_lora_A"], dtype=np.float32)
    lB = np.asarray(inputs["L0_ffn_lora_B"], dtype=np.float32)
    gw = np.asarray(inputs["L0_ffn_gate_w"], dtype=np.float32)
    gb = np.asarray(inputs["L0_ffn_gate_b"], dtype=np.float32)
    x = np.asarray(inputs["L0_attn_x"], dtype=np.float32)

    def strips(Wm):
        return np.ascontiguousarray(
            Wm.T.reshape(NCH, 128, NH, 128).transpose(2, 1, 0, 3)
        ).reshape(NH, 128, D)

    wqs = strips(W_q)
    wkvs = strips(W_kv)
    wot = np.ascontiguousarray(W_o.T)
    wt = np.ascontiguousarray(W.T)
    A_allT = lA.reshape(128, D).T
    comb = np.concatenate([A_allT.reshape(NCH, 128, 128),
                           gw.T.reshape(NCH, 128, 8)], axis=2)
    agw = np.ascontiguousarray(comb.transpose(1, 0, 2)).reshape(128, NCH * 136)
    bbt = np.ascontiguousarray(lB.transpose(0, 2, 1)).reshape(128, D)
    brow4 = np.ascontiguousarray(b.reshape(4, 512))
    gb16 = np.zeros((16, 136), dtype=np.float32)
    gb16[0, 128:136] = gb

    shared = dict(wqs=wqs, wkvs=wkvs, wot=wot, wt=wt, agw=agw, bbt=bbt,
                  brow4=brow4, gb16=gb16,
                  cmat=cs["cmat"], msin=cs["msin"], ident=cs["ident"],
                  hblk=cs["hblk"], sels=cs["sels"], sel4=cs["sel4"])
    in_maps = []
    for core in range(NCORES):
        m = dict(shared)
        m["x"] = np.ascontiguousarray(x[core])
        in_maps.append(m)
    return in_maps


def kernel(**inputs):
    nc = _get_nc()
    in_maps = make_in_maps(inputs)
    res = run_bass_kernel_spmd(nc, in_maps, core_ids=list(range(NCORES)))
    out = np.stack([np.asarray(r["out"], dtype=np.float32)
                    for r in res.results], axis=0)
    return out
